# version 01
# baseline (speedup 1.0000x reference)
import contextlib
import numpy as np
import ml_dtypes

import concourse.bass as bass
import concourse.mybir as mybir
from concourse.bass_utils import run_bass_kernel_spmd

F32 = mybir.dt.float32
BF16 = mybir.dt.bfloat16
AF = mybir.ActivationFunctionType
ALU = mybir.AluOpType
AX = mybir.AxisListType
BF = ml_dtypes.bfloat16

D = 1024
T = 2048
DC = 8
H = 2048
HC = 16
E = 8
FPAD = 640
FC = 5
NHEAD = 16
HD = 64
EPS_A = 1e-8
EPS_LN = 1e-5

DEV_PHASES = None
DEV_NTB = None
DEV_CUT = "full"
DEV_LNCUT = None
DEV_ATTNDBG = False

_CONST_CACHE = {}


def host_consts():
    if _CONST_CACHE:
        return _CONST_CACHE
    d = np.arange(D, dtype=np.float64)[:, None]
    f = np.arange(FPAD, dtype=np.float64)[None, :]
    ang = 2.0 * np.pi * d * f / D
    valid = (f < D // 2 + 1).astype(np.float64)
    C = (np.cos(ang) * valid).astype(np.float32)
    S = (-np.sin(ang) * valid).astype(np.float32)
    w = np.where((f[0] == 0) | (f[0] == D // 2), 1.0, 2.0) * valid[0]
    Ci = ((w[:, None] / D) * np.cos(ang.T)).astype(np.float32)
    Si = (-(w[:, None] / D) * np.sin(ang.T)).astype(np.float32)

    freqs = (10000.0 ** (-(np.arange(0, HD, 2, dtype=np.float32)) / np.float32(HD))).astype(np.float32)
    ang2 = (np.arange(T, dtype=np.float32)[:, None] * freqs[None, :]).astype(np.float32)
    cos32 = np.cos(ang2).astype(np.float32).T
    sin32 = np.sin(ang2).astype(np.float32).T
    cos64 = np.concatenate([cos32, cos32], 0)
    sin64 = np.concatenate([sin32, sin32], 0)
    csT = np.concatenate([cos64, sin64], 0)
    scT = np.concatenate([sin64, cos64], 0)

    ident = np.eye(128, dtype=np.float32)
    yy = np.arange(896)[None, :]
    ii = np.arange(128)[:, None]
    maskT = np.where(yy - 384 < ii, np.float32(-100000.0), np.float32(0.0)).astype(np.float32)

    perm = np.zeros(D, dtype=np.int64)
    for head in range(NHEAD):
        c, hp = divmod(head, 2)
        for h in range(HD):
            pos = c * 128 + (h % 2) * 64 + hp * 32 + (h // 2)
            perm[pos] = head * HD + h
    _CONST_CACHE.update(dict(C=C, S=S, Ci=Ci, Si=Si, csT=csT, scT=scT,
                             ident=ident, maskT=maskT, perm=perm))
    return _CONST_CACHE


def prep_inputs(inputs):
    cc = host_consts()
    perm = cc["perm"]
    g = lambda k: np.asarray(inputs[k], dtype=np.float32)
    m = {}
    m["aWqT"] = np.ascontiguousarray(g("L0_attn_a_Wq").T)
    m["aWkT"] = np.ascontiguousarray(g("L0_attn_a_Wk").T)
    m["aWvT"] = np.ascontiguousarray(g("L0_attn_a_Wv").T)
    m["dftC"] = cc["C"]; m["dftS"] = cc["S"]
    m["dftCi"] = cc["Ci"]; m["dftSi"] = cc["Si"]
    m["bWqT"] = np.ascontiguousarray(g("L0_attn_b_Wq").T[:, perm])
    m["bWkT"] = np.ascontiguousarray(g("L0_attn_b_Wk").T[:, perm])
    m["bWvT"] = np.ascontiguousarray(g("L0_attn_b_Wv").T)
    m["bWoT"] = np.ascontiguousarray(g("L0_attn_b_Wo").T)
    m["bq"] = np.ascontiguousarray(g("L0_attn_b_bq")[perm][None, :])
    m["bk"] = np.ascontiguousarray(g("L0_attn_b_bk")[perm][None, :])
    m["bv"] = np.ascontiguousarray(g("L0_attn_b_bv")[None, :])
    m["bo"] = np.ascontiguousarray(g("L0_attn_b_bo")[None, :])
    m["scale"] = np.ascontiguousarray(g("L0_attn_b_scale").reshape(1, 1))
    m["cos64"] = cc["csT"]; m["sin64"] = cc["scT"]
    m["maskT"] = cc["maskT"]
    m["rw"] = g("L0_ffn_router_w")
    m["rb"] = np.ascontiguousarray(g("L0_ffn_router_b")[None, :])
    m["sw1"] = g("L0_ffn_sw1")
    m["sb1T"] = np.ascontiguousarray(g("L0_ffn_sb1").reshape(HC, 128).T)
    m["sw2"] = g("L0_ffn_sw2")
    m["sb2"] = np.ascontiguousarray(g("L0_ffn_sb2")[None, :])
    m["rw1"] = g("L0_ffn_rw1")
    m["rb1T"] = np.ascontiguousarray(g("L0_ffn_rb1").reshape(E, HC, 128).transpose(0, 2, 1))
    m["rw2"] = g("L0_ffn_rw2")
    m["rb2"] = g("L0_ffn_rb2")
    m["ident"] = cc["ident"]
    x = np.asarray(inputs["L0_attn_a_x"], dtype=np.float32)
    xts = [np.ascontiguousarray(x[i].T) for i in range(8)]
    return m, xts


ENGS = ("tensor", "vector", "scalar", "gpsimd", "sync")
_OWN = {"tensor": "pe", "vector": "dv", "scalar": "ac", "gpsimd": "gp", "sync": "ds"}


class Prog:
    def __init__(self):
        self.ops = {e: [] for e in ENGS}
        self.cnt = {"pe": 0, "dv": 0, "ac": 0, "gp": 0, "ds": 0, "dg": 0}
        self.waited = {e: {} for e in ENGS}

    def emit(self, eng, fn, waits=(), inc=None, amt=1):
        real = []
        for w in waits:
            if w is None:
                continue
            s, v = w
            d = self.waited[eng]
            if d.get(s, -1) < v:
                d[s] = v
                real.append((s, v))
        tok = None
        if inc is not None:
            self.cnt[inc] += amt
            tok = (inc, self.cnt[inc])
        self.ops[eng].append((fn, real, inc, amt))
        return tok

    def barrier(self, marker_fns=None):
        if marker_fns:
            for e, fn in marker_fns.items():
                self.emit(e, fn, inc=_OWN[e])
        snap = [(s, self.cnt[s]) for s in self.cnt]
        for e in ENGS:
            self.emit(e, lambda eng: None, waits=snap)
        return snap

    def realize(self, block, sems):
        for eng_name in ENGS:
            ops = self.ops[eng_name]

            def body(engine, ops=ops):
                for (fn, waits, inc, amt) in ops:
                    for (s, v) in waits:
                        engine.wait_ge(sems[s], v)
                    inst = fn(engine)
                    if inc is not None:
                        inst.then_inc(sems[inc], amt)
            getattr(block, eng_name)(body)


class Banks:
    def __init__(self):
        self.readers = [[] for _ in range(8)]
        self.writer = [None] * 8

    def write_waits(self, b):
        w = list(self.readers[b])
        if self.writer[b] is not None:
            w.append(self.writer[b])
        return w

    def wrote(self, b, tok):
        self.writer[b] = tok
        self.readers[b] = []

    def read(self, b, tok):
        self.readers[b].append(tok)


def build_nc(phases=("attn_a", "ffn1", "attn_b", "ffn2"), dev_dump=()):
    nc = bass.Bass("TRN2", target_bir_lowering=False)
    P = Prog()
    BK = Banks()

    def din(name, shape, dtype=F32):
        return nc.dram_tensor(name, list(shape), dtype, kind="ExternalInput").ap()

    x_t = din("x_t", (D, T))
    aWT = [din(n, (D, D)) for n in ("aWqT", "aWkT", "aWvT")]
    dftC = din("dftC", (D, FPAD)); dftS = din("dftS", (D, FPAD))
    dftCi = din("dftCi", (FPAD, D)); dftSi = din("dftSi", (FPAD, D))
    bWqT = din("bWqT", (D, D)); bWkT = din("bWkT", (D, D))
    bWvT = din("bWvT", (D, D)); bWoT = din("bWoT", (D, D))
    bq = din("bq", (1, D)); bk = din("bk", (1, D)); bv = din("bv", (1, D)); bo = din("bo", (1, D))
    scale_in = din("scale", (1, 1))
    cos64_in = din("cos64", (128, T)); sin64_in = din("sin64", (128, T))
    maskT_in = din("maskT", (128, 896))
    rw_in = din("rw", (D, E)); rb_in = din("rb", (1, E))
    sw1_in = din("sw1", (D, H)); sb1T_in = din("sb1T", (128, HC))
    sw2_in = din("sw2", (H, D)); sb2_in = din("sb2", (1, D))
    rw1_in = din("rw1", (E, D, H)); rb1T_in = din("rb1T", (E, 128, HC))
    rw2_in = din("rw2", (E, H, D)); rb2_in = din("rb2", (E, D))
    ident_in = din("ident", (128, 128))

    out_t = nc.dram_tensor("out_t", [D, T], F32, kind="ExternalOutput").ap()
    wdram = nc.dram_tensor("wdram", [128, 128], F32, kind="Internal").ap()
    lnxdram = nc.dram_tensor("lnxdram", [128, DC * T], BF16, kind="Internal").ap()

    dumps = {}
    for name, shape in dev_dump:
        dumps[name] = nc.dram_tensor("dump_" + name, list(shape), F32,
                                     kind="ExternalOutput").ap()

    with contextlib.ExitStack() as ctx:
        ent = ctx.enter_context
        R = ent(nc.sbuf_tensor("R", [128, DC * T], F32))
        ABF = ent(nc.sbuf_tensor("ABF", [128, 56320], BF16))
        AFP = ent(nc.sbuf_tensor("AFP", [128, 4608], F32))
        ident_sb = ent(nc.sbuf_tensor("ident_sb", [128, 128], F32))
        ones_b = ent(nc.sbuf_tensor("ones_b", [1, 128], BF16))
        ones_f = ent(nc.sbuf_tensor("ones_f", [1, 128], F32))
        onescol_b = ent(nc.sbuf_tensor("onescol_b", [128, 1], BF16))
        nones_b = ent(nc.sbuf_tensor("nones_b", [1, 128], BF16))
        ROWB = ent(nc.sbuf_tensor("ROWB", [1, 5120], BF16))
        rows = AFP[0:1, 2560:4608]
        sml_b = ent(nc.sbuf_tensor("sml_b", [128, 1280], BF16))
        bar_f = ent(nc.sbuf_tensor("bar_f", [128, 4], F32))
        ps = [ent(nc.psum_tensor(f"ps{i}", [128, 512], F32)) for i in range(8)]
        sems = {s: ent(nc.semaphore(s)) for s in P.cnt}
        block = ent(nc.Block())

        def dma(queue, out, in_, waits=()):
            eng = "sync" if queue == "ds" else "gpsimd"
            return P.emit(eng, lambda e, o=out, i=in_: e.dma_start(out=o, in_=i),
                          waits=waits, inc=queue, amt=16)

        def vop(fn, waits=(), eng="vector", inc=None):
            inc = {"vector": "dv", "scalar": "ac", "tensor": "pe"}[eng] if inc is None else inc
            return P.emit(eng, fn, waits=waits, inc=inc)

        class MMGroup:
            def __init__(self, bank, region, waits=()):
                self.bank = bank
                self.out = region
                self.first = True
                self.tok = None
                self.waits = list(waits) + BK.write_waits(bank)

            def mm(self, lhsT, rhs, stop=False, extra_waits=(), tag=False):
                w = (self.waits + list(extra_waits)) if self.first else list(extra_waits)
                tok = P.emit("tensor",
                             lambda e, o=self.out, l=lhsT, r=rhs, st=self.first, sp=stop:
                             e.matmul(o, l, r, start=st, stop=sp), waits=w,
                             inc=("pe" if (stop or tag) else None))
                self.first = False
                if stop:
                    self.tok = tok
                return tok

            def done(self):
                assert self.tok is not None, "group not closed with stop=True"
                BK.wrote(self.bank, self.tok)
                return self.tok

        def evac(bank, tok_mm, fn, eng="vector", waits=()):
            tok = vop(fn, waits=[tok_mm] + list(waits), eng=eng)
            BK.read(bank, tok)
            return tok

        def full_barrier():
            return P.barrier({
                "vector": lambda e: e.memset(bar_f[:, 0:1], 0.0),
                "scalar": lambda e: e.memzero(bar_f[:, 1:2]),
                "gpsimd": lambda e: e.memset(bar_f[:, 2:3], 0.0),
            })

        t_id = dma("ds", ident_sb[:, :], ident_in)
        vop(lambda e: e.memset(ones_b[:, :], 1.0))
        vop(lambda e: e.memset(ones_f[:, :], 1.0))
        vop(lambda e: e.memset(nones_b[:, :], -1.0))
        t_ones = vop(lambda e: e.memset(onescol_b[:, :], 1.0))

        Rv = R[:, :].rearrange("p (c t) -> p c t", c=DC)
        t_x = dma("ds", Rv, x_t.rearrange("(c p) t -> p c t", p=128))
        t_R = t_x

        class Arena:
            def __init__(self, tile, total):
                self.tile, self.total, self.off = tile, total, 0

            def alloc(self, n):
                assert self.off + n <= self.total, f"arena overflow {self.off}+{n}>{self.total}"
                lo = self.off
                self.off += n
                return lo

            def reset(self):
                self.off = 0

        AB = Arena(ABF, 56320)

        def layernorm(tok0, nb, lnx0, stride, deps):
            assert nb <= 512
            half = 512
            g1w = BK.write_waits(6) + [t_ones]
            mm_markers = []
            for c in range(DC):
                buf = (c % 2)
                dst = sml_b[:, buf * half:buf * half + nb]
                w = list(deps)
                if len(mm_markers) >= 2:
                    w.append(mm_markers[-2])
                cp = vop(lambda e, o=dst, i=R[:, c * T + tok0:c * T + tok0 + nb]:
                         e.tensor_copy(o, i), waits=w)
                mm_markers.append(P.emit(
                    "tensor",
                    lambda e, o=ps[6][0:1, 0:nb], l=onescol_b[:, :], r=dst,
                    st=(c == 0), sp=(c == DC - 1):
                    e.matmul(o, l, r, start=st, stop=sp),
                    waits=(g1w + [cp]) if c == 0 else [cp], inc="pe"))
            t_s1 = mm_markers[-1]
            BK.wrote(6, t_s1)
            if DEV_LNCUT == "s1":
                return t_s1
            g2w = BK.write_waits(7) + [t_ones]
            mm_markers2 = []
            for c in range(DC):
                buf = (c % 2)
                dst = sml_b[:, buf * half:buf * half + nb]
                w = [t_s1] if c < 2 else [mm_markers2[-2]]
                sq = vop(lambda e, o=dst, a=R[:, c * T + tok0:c * T + tok0 + nb]:
                         e.tensor_mul(o, a, a), waits=w + list(deps))
                mm_markers2.append(P.emit(
                    "tensor",
                    lambda e, o=ps[7][0:1, 0:nb], l=onescol_b[:, :], r=dst,
                    st=(c == 0), sp=(c == DC - 1):
                    e.matmul(o, l, r, start=st, stop=sp),
                    waits=(g2w + [sq]) if c == 0 else [sq], inc="pe"))
            t_s2 = mm_markers2[-1]
            BK.wrote(7, t_s2)
            if DEV_LNCUT == "s2":
                return t_s2
            t1 = evac(6, t_s1, lambda e, o=rows[:, 0:nb], i=ps[6][0:1, 0:nb]:
                      e.tensor_scalar(o, i, 1.0 / D, None, op0=ALU.mult))
            t2 = evac(7, t_s2, lambda e, o=rows[:, nb:2 * nb], i=ps[7][0:1, 0:nb]:
                      e.tensor_scalar(o, i, 1.0 / D, None, op0=ALU.mult))
            if DEV_LNCUT == "r1":
                return t2
            t3 = vop(lambda e, o=rows[:, 2 * nb:3 * nb], a=rows[:, 0:nb]:
                     e.tensor_mul(o, a, a), waits=[t1])
            t4 = vop(lambda e, o=rows[:, nb:2 * nb], a=rows[:, nb:2 * nb],
                     b=rows[:, 2 * nb:3 * nb]: e.tensor_sub(o, a, b), waits=[t2, t3])
            t4b = vop(lambda e, o=rows[:, nb:2 * nb], i=rows[:, nb:2 * nb]:
                      e.tensor_scalar(o, i, EPS_LN, None, op0=ALU.add), waits=[t4])
            if DEV_LNCUT == "r2":
                return t4b
            t5 = vop(lambda e, o=rows[:, 2 * nb:3 * nb], i=rows[:, nb:2 * nb]:
                     e.activation(out=o, in_=i, func=AF.Sqrt),
                     waits=[t4b], eng="scalar")
            t6 = vop(lambda e, o=rows[:, 2 * nb:3 * nb], i=rows[:, 2 * nb:3 * nb]:
                     e.reciprocal(o, i), waits=[t5])
            if DEV_LNCUT == "r3":
                return t6
            t7 = vop(lambda e, o=rows[:, 3 * nb:4 * nb], a=rows[:, 0:nb],
                     b=rows[:, 2 * nb:3 * nb]: e.tensor_mul(o, a, b), waits=[t6])
            t8 = vop(lambda e, o=rows[:, 3 * nb:4 * nb], i=rows[:, 3 * nb:4 * nb]:
                     e.tensor_scalar(o, i, -1.0, None, op0=ALU.mult), waits=[t7])
            if DEV_LNCUT == "rows":
                return t8
            gb1 = MMGroup(6, ps[6][:, 0:nb], waits=[t6, t8])
            gb1.mm(ones_f[0:1, :], rows[:, 2 * nb:3 * nb], stop=True)
            tb1 = gb1.done()
            gb2 = MMGroup(7, ps[7][:, 0:nb], waits=[t8])
            gb2.mm(ones_f[0:1, :], rows[:, 3 * nb:4 * nb], stop=True)
            tb2 = gb2.done()
            tc1 = evac(6, tb1, lambda e, o=AFP[:, 0:nb], i=ps[6][:, 0:nb]:
                       e.tensor_copy(o, i))
            tc2 = evac(7, tb2, lambda e, o=AFP[:, 512:512 + nb], i=ps[7][:, 0:nb]:
                       e.tensor_copy(o, i))
            if DEV_LNCUT == "bcast":
                return tc2
            last = None
            for c in range(DC):
                tt = vop(lambda e, o=AFP[:, 1024:1024 + nb],
                         a=R[:, c * T + tok0:c * T + tok0 + nb], b=AFP[:, 0:nb]:
                         e.tensor_mul(o, a, b), waits=[tc1, tc2] + list(deps))
                dst = ABF[:, lnx0 + c * stride:lnx0 + c * stride + nb]
                last = vop(lambda e, o=dst, a=AFP[:, 1024:1024 + nb],
                           b=AFP[:, 512:512 + nb]: e.tensor_add(o, a, b), waits=[tt])
            return last

        def phase_attn_a():
            AB.reset()
            TBA = 256
            NTB = DEV_NTB if DEV_NTB is not None else T // TBA
            lnx0 = AB.alloc(DC * TBA)
            w0 = AB.alloc(3 * DC * D)
            cs0 = AB.alloc(2 * DC * FPAD)
            ci0 = AB.alloc(2 * 2 * FPAD)
            qkv0 = AB.alloc(3 * DC * TBA)
            z0 = AB.alloc(2 * FC * TBA)

            wt = []
            for i, w in enumerate(aWT):
                dst = ABF[:, w0 + i * DC * D:w0 + (i + 1) * DC * D]
                wt.append(dma("dg", dst.rearrange("p (c e) -> p c e", c=DC),
                              w.rearrange("(c p) e -> p c e", p=128)))
            for i, w in enumerate((dftC, dftS)):
                dst = ABF[:, cs0 + i * DC * FPAD:cs0 + (i + 1) * DC * FPAD]
                wt.append(dma("dg", dst.rearrange("p (c f) -> p c f", c=DC),
                              w.rearrange("(c p) f -> p c f", p=128)))
            t_w = wt[-1]

            t_zero = vop(lambda e: e.memset(AFP[:, 2048:2058], 0.0))
            carry_tok = [t_zero] * FC

            SP = {"kr": 0, "ki": 256, "vr": 512, "vi": 768, "qr": 1024,
                  "qi": 1280, "t1": 1536, "t2": 1792}
            spv = lambda n: AFP[:, SP[n]:SP[n] + TBA]
            qv = lambda s, ec: ABF[:, qkv0 + s * DC * TBA + ec * TBA:
                                   qkv0 + s * DC * TBA + (ec + 1) * TBA]
            zv = lambda comp, fc: ABF[:, z0 + (comp * FC + fc) * TBA:
                                      z0 + (comp * FC + fc + 1) * TBA]

            t_qkv_prev = None
            ci_last_mm = [None, None]
            for itb in range(NTB):
                tok0 = itb * TBA
                deps = [t_R]
                if t_qkv_prev is not None:
                    deps.append(t_qkv_prev)
                t_ln = layernorm(tok0, TBA, lnx0, TBA, deps)
                lnx = lambda c: ABF[:, lnx0 + c * TBA:lnx0 + (c + 1) * TBA]
                if DEV_CUT == "ln":
                    t_qkv_prev = t_ln
                    continue

                last_evacs = []
                for s in range(3):
                    for ec in range(DC):
                        bank = (s * DC + ec) % 2
                        g = MMGroup(bank, ps[bank][:, 0:TBA], waits=[t_ln, t_w])
                        for dc in range(DC):
                            l = ABF[:, w0 + s * DC * D + dc * D + ec * 128:
                                    w0 + s * DC * D + dc * D + (ec + 1) * 128]
                            g.mm(l, lnx(dc), stop=(dc == DC - 1))
                        tm = g.done()
                        te = evac(bank, tm,
                                  lambda e, o=qv(s, ec), i=ps[bank][:, 0:TBA]:
                                  e.activation(out=o, in_=i, func=AF.Copy),
                                  eng="scalar")
                        last_evacs.append(te)
                t_qkv = last_evacs[-1]
                t_qkv_prev = t_qkv
                if DEV_CUT == "proj":
                    continue

                t_z = None
                for fc in range(FC):
                    specs = [("kr", 1, 0), ("ki", 1, 1), ("vr", 2, 0),
                             ("vi", 2, 1), ("qr", 0, 0), ("qi", 0, 1)]
                    ev = {}
                    for bi, (nm, sig, comp) in enumerate(specs):
                        g = MMGroup(bi, ps[bi][:, 0:TBA], waits=[t_qkv])
                        for ec in range(DC):
                            l = ABF[:, cs0 + comp * DC * FPAD + ec * FPAD + fc * 128:
                                    cs0 + comp * DC * FPAD + ec * FPAD + (fc + 1) * 128]
                            g.mm(l, qv(sig, ec), stop=(ec == DC - 1))
                        tm = g.done()
                        ev[nm] = evac(bi, tm,
                                      lambda e, o=spv(nm), i=ps[bi][:, 0:TBA]:
                                      e.tensor_copy(o, i))
                    tA = vop(lambda e, o=spv("t1"), a=spv("kr"): e.tensor_mul(o, a, a),
                             waits=[ev["kr"]])
                    tB = vop(lambda e, o=spv("t2"), a=spv("ki"): e.tensor_mul(o, a, a),
                             waits=[ev["ki"]])
                    tC = vop(lambda e, o=spv("t1"), a=spv("t1"), b=spv("t2"):
                             e.tensor_add(o, a, b), waits=[tA, tB])
                    tD = vop(lambda e, o=spv("t1"), i=spv("t1"):
                             e.activation(out=o, in_=i, func=AF.Sqrt),
                             waits=[tC], eng="scalar")
                    tE = vop(lambda e, o=spv("t1"), i=spv("t1"):
                             e.tensor_scalar(o, i, EPS_A, None, op0=ALU.add), waits=[tD])
                    tF = vop(lambda e, o=spv("t1"), i=spv("t1"): e.reciprocal(o, i),
                             waits=[tE])
                    tG = vop(lambda e, o=spv("kr"), a=spv("kr"), b=spv("t1"):
                             e.tensor_mul(o, a, b), waits=[tF])
                    tH = vop(lambda e, o=spv("ki"), a=spv("ki"), b=spv("t1"):
                             e.tensor_mul(o, a, b), waits=[tG])
                    tI = vop(lambda e, o=spv("t1"), a=spv("kr"), b=spv("vr"):
                             e.tensor_mul(o, a, b), waits=[tH, ev["vr"]])
                    tJ = vop(lambda e, o=spv("t2"), a=spv("ki"), b=spv("vi"):
                             e.tensor_mul(o, a, b), waits=[tI, ev["vi"]])
                    tK = vop(lambda e, o=spv("t2"), a=spv("t1"), b=spv("t2"):
                             e.tensor_sub(o, a, b), waits=[tJ])
                    tL = vop(lambda e, o=spv("t1"), a=spv("kr"), b=spv("vi"):
                             e.tensor_mul(o, a, b), waits=[tK])
                    tM = vop(lambda e, o=spv("vi"), a=spv("ki"), b=spv("vr"):
                             e.tensor_mul(o, a, b), waits=[tL])
                    tN = vop(lambda e, o=spv("t1"), a=spv("t1"), b=spv("vi"):
                             e.tensor_add(o, a, b), waits=[tM])
                    c0 = AFP[:, 2048 + fc:2048 + fc + 1]
                    c1 = AFP[:, 2048 + FC + fc:2048 + FC + fc + 1]
                    tO = vop(lambda e, o=spv("vr"), d0=spv("t2"), i=c0:
                             e.tensor_tensor_scan(o, d0, d0, i, op0=ALU.add,
                                                  op1=ALU.bypass),
                             waits=[tN, carry_tok[fc]])
                    tP = vop(lambda e, o=spv("vi"), d0=spv("t1"), i=c1:
                             e.tensor_tensor_scan(o, d0, d0, i, op0=ALU.add,
                                                  op1=ALU.bypass), waits=[tO])
                    ct0 = vop(lambda e, o=c0, i=spv("vr")[:, TBA - 1:TBA]:
                              e.tensor_copy(o, i), waits=[tO])
                    ct1 = vop(lambda e, o=c1, i=spv("vi")[:, TBA - 1:TBA]:
                              e.tensor_copy(o, i), waits=[tP])
                    carry_tok[fc] = ct1
                    uA = vop(lambda e, o=spv("t1"), a=spv("qr"): e.tensor_mul(o, a, a),
                             waits=[ev["qr"], ct0, ct1])
                    uB = vop(lambda e, o=spv("t2"), a=spv("qi"): e.tensor_mul(o, a, a),
                             waits=[ev["qi"], uA])
                    uC = vop(lambda e, o=spv("t1"), a=spv("t1"), b=spv("t2"):
                             e.tensor_add(o, a, b), waits=[uB])
                    uD = vop(lambda e, o=spv("t1"), i=spv("t1"):
                             e.activation(out=o, in_=i, func=AF.Sqrt),
                             waits=[uC], eng="scalar")
                    uE = vop(lambda e, o=spv("t1"), i=spv("t1"):
                             e.tensor_scalar(o, i, EPS_A, None, op0=ALU.add), waits=[uD])
                    uF = vop(lambda e, o=spv("t1"), i=spv("t1"): e.reciprocal(o, i),
                             waits=[uE])
                    uG = vop(lambda e, o=spv("qr"), a=spv("qr"), b=spv("t1"):
                             e.tensor_mul(o, a, b), waits=[uF])
                    uH = vop(lambda e, o=spv("qi"), a=spv("qi"), b=spv("t1"):
                             e.tensor_mul(o, a, b), waits=[uG])
                    vA = vop(lambda e, o=spv("t1"), a=spv("vr"), b=spv("qr"):
                             e.tensor_mul(o, a, b), waits=[uH])
                    vB = vop(lambda e, o=spv("t2"), a=spv("vi"), b=spv("qi"):
                             e.tensor_mul(o, a, b), waits=[vA])
                    zw = []
                    if fc == 0 and itb > 0:
                        zw = [t for t in ci_last_mm if t is not None]
                    vC = vop(lambda e, o=zv(0, fc), a=spv("t1"), b=spv("t2"):
                             e.tensor_add(o, a, b), waits=[vB] + zw)
                    vD = vop(lambda e, o=spv("t1"), a=spv("vi"), b=spv("qr"):
                             e.tensor_mul(o, a, b), waits=[vC])
                    vE = vop(lambda e, o=spv("t2"), a=spv("vr"), b=spv("qi"):
                             e.tensor_mul(o, a, b), waits=[vD])
                    t_z = vop(lambda e, o=zv(1, fc), a=spv("t1"), b=spv("t2"):
                              e.tensor_sub(o, a, b), waits=[vE])

                if DEV_CUT == "spec":
                    continue
                ci_mm_tok = None
                for dc in range(DC):
                    pp = dc % 2
                    off = ci0 + pp * 2 * FPAD
                    dmaw = [ci_last_mm[pp]] if ci_last_mm[pp] is not None else []
                    tci = dma("dg",
                              ABF[:, off:off + FPAD].rearrange("p (c x) -> p c x", c=FC),
                              dftCi.rearrange("(c p) x -> p c x", p=128)[:, :, dc * 128:(dc + 1) * 128],
                              waits=dmaw)
                    tsi = dma("dg",
                              ABF[:, off + FPAD:off + 2 * FPAD].rearrange("p (c x) -> p c x", c=FC),
                              dftSi.rearrange("(c p) x -> p c x", p=128)[:, :, dc * 128:(dc + 1) * 128],
                              waits=dmaw)
                    bank = dc % 2
                    g = MMGroup(bank, ps[bank][:, 0:TBA], waits=[t_z, tci, tsi])
                    for comp in range(2):
                        for fc in range(FC):
                            l = ABF[:, off + comp * FPAD + fc * 128:
                                    off + comp * FPAD + (fc + 1) * 128]
                            g.mm(l, zv(comp, fc), stop=(comp == 1 and fc == FC - 1))
                    tm = g.done()
                    ci_last_mm[pp] = tm
                    rdst = R[:, dc * T + tok0:dc * T + tok0 + TBA]
                    evac(bank, tm, lambda e, o=rdst, a=ps[bank][:, 0:TBA], b=rdst:
                         e.tensor_add(o, a, b))
            return full_barrier()


        ffn_state = {"n": 0}

        def phase_ffn():
            ffn_state.clear()
            ffn_state.update({
                "wrow_war": [], "wb_war": [[], []],
                "h1g_war": [[], []], "h1p_war": [], "h1p_rdy": [None] * HC,
                "lnx_users": [[], []], "lnx_dma0": [None, None],
                "lnwar": [None, None], "wA": {}, "wB": {},
            })
            AB.reset()
            TBF = 512
            NTBF = T // TBF
            lnx0 = AB.alloc(2 * DC * TBF)
            slotA = AB.alloc(DC * H)
            slotB = AB.alloc(HC * D)
            h1p0 = AB.alloc(HC * TBF)
            h1g0 = AB.alloc(2 * TBF)
            wb0 = AB.alloc(2 * TBF)
            bias0 = AB.alloc(16 * 10)
            rb20 = AB.alloc(D)
            rwr0 = AB.alloc(DC * E)
            wfm0 = AB.alloc(T)
            assert AB.off <= 56320, AB.off

            t_sw1 = None
            for h4 in range(2):
                t_sw1 = dma("dg",
                            ABF[:, slotA + h4 * DC * H // 2:slotA + (h4 + 1) * DC * H // 2]
                            .rearrange("p (c h) -> p c h", c=DC // 2),
                            sw1_in.rearrange("(c p) h -> p c h", p=128)[:, h4 * DC // 2:(h4 + 1) * DC // 2, :])
            t_sw2 = None
            for h4 in range(2):
                t_sw2 = dma("dg",
                            ABF[:, slotB + h4 * HC * D // 2:slotB + (h4 + 1) * HC * D // 2]
                            .rearrange("p (c h) -> p c h", c=HC // 2),
                            sw2_in.rearrange("(c p) d -> p c d", p=128)[:, h4 * HC // 2:(h4 + 1) * HC // 2, :])
            t_b1 = dma("dg", ABF[:, bias0:bias0 + 16], sb1T_in)
            for e in range(E):
                t_b1 = dma("dg", ABF[:, bias0 + 16 * (1 + e):bias0 + 16 * (2 + e)],
                           rb1T_in[e])
            t_rb2 = dma("dg", ABF[0:8, rb20:rb20 + D], rb2_in)
            t_sb2r = dma("dg", ROWB[0:1, 2048:2048 + D], sb2_in)
            t_rwr = dma("dg", ABF[:, rwr0:rwr0 + DC * E].rearrange("p (c e) -> p c e", c=DC),
                        rw_in.rearrange("(c p) e -> p c e", p=128))
            t_rbr = dma("dg", ROWB[0:1, 4608:4608 + E], rb_in)
            t_ones_row = vop(lambda e: e.memset(ROWB[0:1, 4096:4608], 1.0))

            lnxbuf = lambda b, c: ABF[:, lnx0 + b * DC * TBF + c * TBF:
                                      lnx0 + b * DC * TBF + (c + 1) * TBF]
            lnx_dma = [None, None]
            lnx_rdy = [None] * NTBF
            tr_tok = None
            for tb in range(NTBF):
                b = tb % 2
                deps = [t_R, ffn_state["lnwar"][b], lnx_dma[b]]
                t_ln = layernorm(tb * TBF, TBF, lnx0 + b * DC * TBF, TBF, deps)
                lnx_rdy[tb] = t_ln
                lnx_dma[b] = dma("ds",
                                 lnxdram.rearrange("p (c t) -> p c t", c=DC)[:, :, tb * TBF:(tb + 1) * TBF],
                                 ABF[:, lnx0 + b * DC * TBF:lnx0 + (b + 1) * DC * TBF]
                                 .rearrange("p (c t) -> p c t", c=DC),
                                 waits=[t_ln])
                ffn_state["lnx_wr"] = ffn_state.get("lnx_wr", {})
                ffn_state["lnx_wr"][tb] = lnx_dma[b]
                g = MMGroup(4, ps[4][0:8, 0:TBF], waits=[t_ln, t_rwr, t_rbr, t_ones_row])
                for dc in range(DC):
                    g.mm(ABF[:, rwr0 + dc * E:rwr0 + (dc + 1) * E], lnxbuf(b, dc),
                         stop=False)
                g.mm(ROWB[0:1, 4608:4608 + E], ROWB[0:1, 4096:4096 + TBF], stop=True)
                t_lg = g.done()
                ffn_state["lnwar"][b] = t_lg
                t_lge = evac(4, t_lg, lambda e, o=AFP[0:8, 1536:1536 + TBF],
                             i=ps[4][0:8, 0:TBF]: e.tensor_copy(o, i))
                for j in range(4):
                    c = tb * 4 + j
                    w = [t_lge, t_id] + BK.write_waits(5) if (tb == 0 and j == 0) else [t_lge]
                    tr_tok = P.emit("tensor",
                                    lambda e, o=ps[5][0:128, c * 8:(c + 1) * 8],
                                    i=AFP[0:8, 1536 + j * 128:1536 + (j + 1) * 128],
                                    d=ident_sb[0:8, 0:8]: e.transpose(o, i, d),
                                    waits=w, inc="pe")
            BK.wrote(5, tr_tok)

            A = lambda a, b_: AFP[:, a:b_]
            lg_cm = A(0, 128); lg_em = A(128, 256); uu = A(256, 384)
            tt1 = A(384, 512); tt2 = A(512, 640); rep = A(640, 768)
            red = A(768, 832)
            w_em = A(832, 960); wT = A(960, 1088)
            t0 = evac(5, tr_tok, lambda e, o=lg_cm, i=ps[5][:, 0:128]: e.tensor_copy(o, i))
            toks = [t0]
            for e8 in range(E):
                toks.append(vop(lambda e, o=lg_em[:, e8 * 16:(e8 + 1) * 16],
                                i=lg_cm.rearrange("p (c e) -> p e c", e=E)[:, e8, :]:
                                e.tensor_copy(o, i), waits=[toks[-1]]))
            def maxtree(src, dstcol, w):
                a = vop(lambda e, o=tt1[:, 0:64], x=src[:, 0:64], y=src[:, 64:128]:
                        e.tensor_max(o, x, y), waits=w)
                b_ = vop(lambda e, o=tt1[:, 64:96], x=tt1[:, 0:32], y=tt1[:, 32:64]:
                         e.tensor_max(o, x, y), waits=[a])
                return vop(lambda e, o=red[:, dstcol:dstcol + 16], x=tt1[:, 64:80],
                           y=tt1[:, 80:96]: e.tensor_max(o, x, y), waits=[b_])
            def repx(col, w):
                a = vop(lambda e, o=rep[:, 0:16], i=red[:, col:col + 16]:
                        e.tensor_copy(o, i), waits=w)
                b_ = vop(lambda e, o=rep[:, 16:32], i=rep[:, 0:16]: e.tensor_copy(o, i), waits=[a])
                c_ = vop(lambda e, o=rep[:, 32:64], i=rep[:, 0:32]: e.tensor_copy(o, i), waits=[b_])
                return vop(lambda e, o=rep[:, 64:128], i=rep[:, 0:64]: e.tensor_copy(o, i), waits=[c_])
            tm1 = maxtree(lg_em, 0, [toks[-1]])
            trp = repx(0, [tm1])
            tsub = vop(lambda e, o=tt2, a=lg_em, b=rep: e.tensor_sub(o, a, b), waits=[trp])
            texp = vop(lambda e, o=uu, i=tt2: e.activation(out=o, in_=i, func=AF.Exp),
                       waits=[tsub], eng="scalar")
            ta = vop(lambda e, o=tt1[:, 0:64], x=uu[:, 0:64], y=uu[:, 64:128]:
                     e.tensor_add(o, x, y), waits=[texp])
            tb_ = vop(lambda e, o=tt1[:, 64:96], x=tt1[:, 0:32], y=tt1[:, 32:64]:
                      e.tensor_add(o, x, y), waits=[ta])
            tc = vop(lambda e, o=red[:, 16:32], x=tt1[:, 64:80], y=tt1[:, 80:96]:
                     e.tensor_add(o, x, y), waits=[tb_])
            td = vop(lambda e, o=red[:, 32:48], i=red[:, 16:32]: e.reciprocal(o, i),
                     waits=[tc])
            te = repx(32, [td])
            tf = vop(lambda e, o=uu, a=uu, b=rep: e.tensor_mul(o, a, b), waits=[te])
            tm1p = maxtree(uu, 0, [tf])
            trp2 = repx(0, [tm1p])
            tg = vop(lambda e, o=tt2, a=uu, b=rep: e.tensor_tensor(o, a, b, op=ALU.is_ge),
                     waits=[trp2])
            th = vop(lambda e, o=tt2, i=tt2, a=uu: e.scalar_tensor_tensor(
                o, i, -1e9, a, op0=ALU.mult, op1=ALU.add), waits=[tg])
            tm2 = maxtree(tt2, 48, [th])
            trp3 = repx(48, [tm2])
            ti = vop(lambda e, o=tt2, a=uu, b=rep: e.tensor_tensor(o, a, b, op=ALU.is_ge),
                     waits=[trp3])
            tj = vop(lambda e, o=w_em, a=tt2, b=uu: e.tensor_mul(o, a, b), waits=[ti])
            g = MMGroup(4, ps[4][:, 0:128], waits=[tj, t_id])
            P.emit("tensor", lambda e, o=ps[4][:, 0:128], i=w_em, d=ident_sb[:, :]:
                   e.transpose(o, i, d), waits=g.waits, inc="pe")
            twT = (("pe", P.cnt["pe"]))
            BK.wrote(4, twT)
            tk = evac(4, twT, lambda e, o=wT, i=ps[4][:, 0:128]: e.tensor_copy(o, i))
            t_wd = dma("ds", wdram, wT, waits=[tk])
            t_wfm = dma("dg", ABF[0:8, wfm0:wfm0 + T],
                        wdram.rearrange("(e c) t -> e (c t)", e=E), waits=[t_wd])

            ffn_state["lnx_users"] = [[ffn_state["lnwar"][0]], [ffn_state["lnwar"][1]]]
            ffn_state["lnx_dma0"] = [lnx_dma[0], lnx_dma[1]]

            def expert(eid):
                st_ = ffn_state
                wslotA = slotA
                tA = st_["wA"][eid]
                tB = st_["wB"][eid]
                bias_off = bias0 if eid < 0 else bias0 + 16 * (1 + eid)
                if eid >= 0:
                    wrow_waits = [t_wfm] + st_["wrow_war"]
                    t_wrow = dma("dg", ROWB[0:1, 0:T],
                                 wdram.rearrange("(e c) t -> e (c t)", e=E)[eid:eid + 1, :],
                                 waits=wrow_waits)
                    st_["wrow_war"] = []
                last_h1_mm = None
                last_y_mm = None
                for tb in range(NTBF):
                    b = tb % 2
                    h1war = list(st_["h1p_war"])
                    st_["h1p_war"] = []
                    lw = [st_["lnx_wr"][tb]] + st_["lnx_users"][b]
                    if st_["lnx_dma0"][b] is not None:
                        lw.append(st_["lnx_dma0"][b])
                    t_lnx = dma("ds",
                                ABF[:, lnx0 + b * DC * TBF:lnx0 + (b + 1) * DC * TBF]
                                .rearrange("p (c t) -> p c t", c=DC),
                                lnxdram.rearrange("p (c t) -> p c t", c=DC)[:, :, tb * TBF:(tb + 1) * TBF],
                                waits=lw)
                    st_["lnx_users"][b] = []
                    st_["lnx_dma0"][b] = t_lnx
                    if eid >= 0:
                        gw = MMGroup(4, ps[4][:, 0:TBF], waits=[t_wrow])
                        gw.mm(ones_b[0:1, :], ROWB[0:1, tb * TBF:(tb + 1) * TBF], stop=True)
                        t_gw = gw.done()
                        wbt = ABF[:, wb0 + b * TBF:wb0 + (b + 1) * TBF]
                        wb_war = st_["wb_war"][b]
                        t_wb = evac(4, t_gw, lambda e, o=wbt, i=ps[4][:, 0:TBF]:
                                    e.tensor_copy(o, i), waits=wb_war)
                        st_["wb_war"][b] = []
                    for hc in range(HC):
                        bank = hc % 2
                        g = MMGroup(bank, ps[bank][:, 0:TBF], waits=[tA, t_lnx])
                        for dc in range(DC):
                            g.mm(ABF[:, wslotA + dc * H + hc * 128:wslotA + dc * H + (hc + 1) * 128],
                                 lnxbuf(b, dc), stop=(dc == DC - 1))
                        t_mm = g.done()
                        st_["lnx_users"][b].append(t_mm)
                        last_h1_mm = t_mm
                        hb = hc % 2
                        h1gt = ABF[:, h1g0 + hb * TBF:h1g0 + (hb + 1) * TBF]
                        if eid < 0:
                            dst = ABF[:, h1p0 + hc * TBF:h1p0 + (hc + 1) * TBF]
                            tg_ = evac(bank, t_mm,
                                       lambda e, o=dst, i=ps[bank][:, 0:TBF],
                                       bb=ABF[:, bias_off + hc:bias_off + hc + 1]:
                                       e.activation(out=o, in_=i, func=AF.Gelu, bias=bb),
                                       eng="scalar", waits=[t_b1] + h1war)
                            st_["h1p_rdy"][hc] = tg_
                        else:
                            war = st_["h1g_war"][hb]
                            st_["h1g_war"][hb] = []
                            tg_ = evac(bank, t_mm,
                                       lambda e, o=h1gt, i=ps[bank][:, 0:TBF],
                                       bb=ABF[:, bias_off + hc:bias_off + hc + 1]:
                                       e.activation(out=o, in_=i, func=AF.Gelu, bias=bb),
                                       eng="scalar", waits=[t_b1] + war)
                            dst = ABF[:, h1p0 + hc * TBF:h1p0 + (hc + 1) * TBF]
                            ts_ = vop(lambda e, o=dst, a=h1gt,
                                      b=ABF[:, wb0 + b * TBF:wb0 + (b + 1) * TBF]:
                                      e.tensor_mul(o, a, b),
                                      waits=[tg_, t_wb] + h1war)
                            st_["h1g_war"][hb].append(ts_)
                            st_["h1p_rdy"][hc] = ts_
                    for dc in range(DC):
                        bank = 2 + dc % 2
                        g = MMGroup(bank, ps[bank][:, 0:TBF], waits=[tB])
                        extra = "sb2" if eid < 0 else ("rb2" if eid == 0 else None)
                        for hc in range(HC):
                            g.mm(ABF[:, slotB + hc * D + dc * 128:slotB + hc * D + (dc + 1) * 128],
                                 ABF[:, h1p0 + hc * TBF:h1p0 + (hc + 1) * TBF],
                                 stop=(hc == HC - 1 and extra is None),
                                 extra_waits=[st_["h1p_rdy"][hc]])
                        if extra == "sb2":
                            g.mm(ROWB[0:1, 2048 + dc * 128:2048 + (dc + 1) * 128],
                                 ROWB[0:1, 4096:4096 + TBF], stop=True,
                                 extra_waits=[t_sb2r, t_ones_row])
                        elif extra == "rb2":
                            g.mm(ABF[0:8, rb20 + dc * 128:rb20 + (dc + 1) * 128],
                                 ABF[0:8, wfm0 + tb * TBF:wfm0 + (tb + 1) * TBF],
                                 stop=True, extra_waits=[t_rb2, t_wfm])
                        t_mm = g.done()
                        last_y_mm = t_mm
                        rdst = R[:, dc * T + tb * TBF:dc * T + (tb + 1) * TBF]
                        tev = evac(bank, t_mm, lambda e, o=rdst, a=ps[bank][:, 0:TBF],
                                   b=rdst: e.tensor_add(o, a, b))
                        st_["h1p_war"].append(t_mm)
                    if eid >= 0:
                        st_["wb_war"][b].append(last_y_mm)
                        st_["wrow_war"].append(last_y_mm)
                st_["lastA"] = last_h1_mm
                st_["lastB"] = last_y_mm

            ffn_state["wA"][-1] = t_sw1
            ffn_state["wB"][-1] = t_sw2
            order = [-1] + list(range(E))
            for idx, eid in enumerate(order):
                expert(eid)
                if idx + 1 < len(order):
                    nxt = order[idx + 1]
                    tA_toks = []
                    for h4 in range(2):
                        tA_toks.append(dma(
                            "dg",
                            ABF[:, slotA + h4 * DC * H // 2:slotA + (h4 + 1) * DC * H // 2]
                            .rearrange("p (c h) -> p c h", c=DC // 2),
                            rw1_in[nxt].rearrange("(c p) h -> p c h", p=128)[:, h4 * DC // 2:(h4 + 1) * DC // 2, :],
                            waits=[ffn_state["lastA"]]))
                    ffn_state["wA"][nxt] = tA_toks[-1]
                    tB_toks = []
                    for h4 in range(2):
                        tB_toks.append(dma(
                            "dg",
                            ABF[:, slotB + h4 * HC * D // 2:slotB + (h4 + 1) * HC * D // 2]
                            .rearrange("p (c h) -> p c h", c=HC // 2),
                            rw2_in[nxt].rearrange("(c p) d -> p c d", p=128)[:, h4 * HC // 2:(h4 + 1) * HC // 2, :],
                            waits=[ffn_state["lastB"]]))
                    ffn_state["wB"][nxt] = tB_toks[-1]
            return full_barrier()


        def phase_attn_b():
            AB.reset()
            TBB = 512
            NTBB = T // TBB
            ksc0 = AB.alloc(DC * T)
            vtok0 = AB.alloc(16 * D)
            lnx0 = AB.alloc(DC * TBB)
            wpc0 = AB.alloc(2048)
            qr0 = AB.alloc(DC * TBB)
            of0 = AB.alloc(DC * TBB)
            ut0 = AB.alloc(2 * TBB)
            tab0 = AB.alloc(2 * T)
            scr0 = AB.alloc(3 * TBB)
            msk0 = AB.alloc(896 + 128)
            brw0 = AB.alloc(TBB)
            qz0 = AB.alloc(2 * TBB)
            assert AB.off <= 56320, AB.off

            lnxbufB = lambda c: ABF[:, lnx0 + c * TBB:lnx0 + (c + 1) * TBB]
            ksc = lambda ec: ABF[:, ksc0 + ec * T:ksc0 + (ec + 1) * T]
            scr_raw = ABF[:, scr0:scr0 + TBB]
            scr_rope = ABF[:, scr0 + TBB:scr0 + 2 * TBB]
            scr_tt = ABF[:, scr0 + 2 * TBB:scr0 + 3 * TBB]

            dma("dg", ABF[:, tab0:tab0 + T], cos64_in)
            t_tab = dma("dg", ABF[:, tab0 + T:tab0 + 2 * T], sin64_in)
            t_msk = dma("dg", ABF[:, msk0:msk0 + 896], maskT_in)
            t_idb = dma("dg", ABF[:, msk0 + 896:msk0 + 1024], ident_in)
            t_bq = dma("dg", ROWB[0:1, 0:D], bq)
            t_bk = dma("dg", ROWB[0:1, D:2 * D], bk)
            t_bv = dma("dg", ROWB[0:1, 2 * D:3 * D], bv)
            t_bo = dma("dg", ROWB[0:1, 3 * D:4 * D], bo)
            t_or = vop(lambda e: e.memset(ROWB[0:1, 4096:4608], 1.0))
            t_sc0 = dma("ds", AFP[0:1, 2049:2050], scale_in)
            gsc = MMGroup(4, ps[4][:, 0:1], waits=[t_sc0])
            gsc.mm(ones_f[0:1, :], AFP[0:1, 2049:2050], stop=True)
            t_scm = gsc.done()
            t_scl = evac(4, t_scm, lambda e, o=AFP[:, 2048:2049], i=ps[4][:, 0:1]:
                         e.tensor_copy(o, i))

            mks_tok = [None] * NHEAD
            mkc = lambda h: AFP[0:1, 2056 + h:2057 + h]
            qmc = lambda h: AFP[0:1, 2072 + h:2073 + h]

            def rope_assemble(raw_tok, tb0, nt, dst, scaled, war):
                cs = lambda p0, p1: ABF[p0:p1, tab0 + tb0:tab0 + tb0 + nt]
                sc = lambda p0, p1: ABF[p0:p1, tab0 + T + tb0:tab0 + T + tb0 + nt]
                A = scr_rope
                Bc = scr_tt
                w0 = [raw_tok, t_tab] + war
                a1 = vop(lambda e, o=A[0:64, 0:nt], a=cs(0, 64), b=scr_raw[0:64, 0:nt]:
                         e.tensor_mul(o, a, b), waits=w0)
                a2 = vop(lambda e, o=Bc[0:64, 0:nt], a=cs(64, 128), b=scr_raw[64:128, 0:nt]:
                         e.tensor_mul(o, a, b), waits=[a1])
                a3 = vop(lambda e, o=Bc[64:128, 0:nt], a=sc(0, 64), b=scr_raw[0:64, 0:nt]:
                         e.tensor_mul(o, a, b), waits=[a2])
                a4 = vop(lambda e, o=A[64:128, 0:nt], a=sc(64, 128), b=scr_raw[64:128, 0:nt]:
                         e.tensor_mul(o, a, b), waits=[a3])
                a5 = vop(lambda e, o=A[0:64, 0:nt], a=A[0:64, 0:nt], b=Bc[0:64, 0:nt]:
                         e.tensor_sub(o, a, b), waits=[a4])
                a6 = vop(lambda e, o=A[64:128, 0:nt], a=Bc[64:128, 0:nt], b=A[64:128, 0:nt]:
                         e.tensor_add(o, a, b), waits=[a5])
                tok = a6
                if scaled:
                    tok = vop(lambda e, o=A[:, 0:nt], i=A[:, 0:nt],
                              s=AFP[:, 2048:2049]:
                              e.tensor_scalar(o, i, s, None, op0=ALU.mult),
                              waits=[a6, t_scl])
                c1 = vop(lambda e, o=dst[0:32, 0:nt], i=A[0:32, 0:nt]:
                         e.tensor_copy(o, i), waits=[tok])
                c2 = vop(lambda e, o=dst[32:64, 0:nt], i=A[64:96, 0:nt]:
                         e.tensor_copy(o, i), waits=[c1])
                c3 = vop(lambda e, o=dst[64:96, 0:nt], i=A[32:64, 0:nt]:
                         e.tensor_copy(o, i), waits=[c2])
                c4 = vop(lambda e, o=dst[96:128, 0:nt], i=A[96:128, 0:nt]:
                         e.tensor_copy(o, i), waits=[c3])
                return c4

            wpc_war = []
            kv_war = []
            norm_mm_prev = None
            lnwar = None
            lnx_wr = {}
            for tb in range(NTBB):
                t_ln = layernorm(tb * TBB, TBB, lnx0, TBB,
                                 [t_R] + ([lnwar] if lnwar else []))
                lnx_wr[tb] = dma("ds",
                                 lnxdram.rearrange("p (c t) -> p c t", c=DC)[:, :, tb * TBB:(tb + 1) * TBB],
                                 ABF[:, lnx0:lnx0 + DC * TBB].rearrange("p (c t) -> p c t", c=DC),
                                 waits=[t_ln])
                last_mm = None
                for ec in range(DC):
                    pp = ec % 2
                    woff = wpc0 + pp * 1024
                    t_wk = dma("dg",
                               ABF[:, woff:woff + 1024].rearrange("p (c x) -> p c x", c=DC),
                               bWkT.rearrange("(c p) x -> p c x", p=128)[:, :, ec * 128:(ec + 1) * 128],
                               waits=wpc_war)
                    wpc_war = []
                    bank = ec % 2
                    g = MMGroup(bank, ps[bank][:, 0:TBB], waits=[t_ln, t_wk, t_bk, t_or])
                    for dc in range(DC):
                        g.mm(ABF[:, woff + dc * 128:woff + (dc + 1) * 128],
                             lnxbufB(dc), stop=False)
                    g.mm(ROWB[0:1, D + ec * 128:D + (ec + 1) * 128],
                         ROWB[0:1, 4096:4096 + TBB], stop=True)
                    t_mm = g.done()
                    wpc_war.append(t_mm)
                    last_mm = t_mm
                    t_raw = evac(bank, t_mm,
                                 lambda e, o=scr_raw[:, 0:TBB], i=ps[bank][:, 0:TBB]:
                                 e.activation(out=o, in_=i, func=AF.Copy), eng="scalar",
                                 waits=kv_war)
                    kv_war = []
                    dst = ABF[:, ksc0 + ec * T + tb * TBB:ksc0 + ec * T + (tb + 1) * TBB]
                    t_asm = rope_assemble(t_raw, tb * TBB, TBB, dst, False, [])
                    kv_war.append(t_asm)
                    t_sq = vop(lambda e, o=scr_rope[:, 0:TBB], i=dst:
                               e.activation(out=o, in_=i, func=AF.Square),
                               waits=[t_asm], eng="scalar")
                    for hh in range(2):
                        head = 2 * ec + hh
                        gw = BK.write_waits(6) + [t_sq, t_ones]
                        gn = MMGroup(6, ps[6][0:1, 0:TBB], waits=gw)
                        gn.mm(onescol_b[64 * hh:64 * (hh + 1), 0:1],
                              scr_rope[64 * hh:64 * (hh + 1), 0:TBB], stop=True)
                        t_nm = gn.done()
                        t_rm = evac(6, t_nm, lambda e, o=AFP[0:1, 2089:2090],
                                    i=ps[6][0:1, 0:TBB]:
                                    e.reduce_max(o, i, axis=AX.X))
                        if tb == 0:
                            mks_tok[head] = vop(lambda e, o=mkc(head),
                                                i=AFP[0:1, 2089:2090]: e.tensor_copy(o, i),
                                                waits=[t_rm])
                        else:
                            mks_tok[head] = vop(lambda e, o=mkc(head), a=mkc(head),
                                                b=AFP[0:1, 2089:2090]: e.tensor_max(o, a, b),
                                                waits=[t_rm, mks_tok[head]])
                lnwar = last_mm
                for eh in range(4):
                    woff = wpc0
                    t_wv = dma("dg",
                               ABF[:, woff:woff + 2048].rearrange("p (c x) -> p c x", c=DC),
                               bWvT.rearrange("(c p) x -> p c x", p=128)[:, :, eh * 256:(eh + 1) * 256],
                               waits=wpc_war)
                    wpc_war = []
                    for tcj in range(TBB // 128):
                        tc = tb * 4 + tcj
                        bank = tcj % 2
                        g = MMGroup(bank, ps[bank][:, 0:256], waits=[t_ln, t_wv, t_bv])
                        for dc in range(DC):
                            g.mm(ABF[:, lnx0 + dc * TBB + tcj * 128:lnx0 + dc * TBB + (tcj + 1) * 128],
                                 ABF[:, woff + dc * 256:woff + (dc + 1) * 256], stop=False)
                        g.mm(ones_b[0:1, :], ROWB[0:1, 2 * D + eh * 256:2 * D + (eh + 1) * 256],
                             stop=True)
                        t_mm = g.done()
                        wpc_war.append(t_mm)
                        lnwar = t_mm
                        dst = ABF[:, vtok0 + tc * D + eh * 256:vtok0 + tc * D + (eh + 1) * 256]
                        evac(bank, t_mm, lambda e, o=dst, i=ps[bank][:, 0:256]:
                             e.activation(out=o, in_=i, func=AF.Copy), eng="scalar")

            if DEV_ATTNDBG:
                snap0 = P.barrier()
                dma("dg", dumps["ksc"], ABF[:, ksc0:ksc0 + DC * T], waits=snap0)
                dma("dg", dumps["vtok"], ABF[:, vtok0:vtok0 + 16 * D], waits=snap0)
            ut_war = [[], []]
            of_war = []
            brw_war = []
            linv_war = []
            qz_war = []
            for tqb in range(NTBB):
                t_lnx = dma("ds",
                            ABF[:, lnx0:lnx0 + DC * TBB].rearrange("p (c t) -> p c t", c=DC),
                            lnxdram.rearrange("p (c t) -> p c t", c=DC)[:, :, tqb * TBB:(tqb + 1) * TBB],
                            waits=[lnx_wr[tqb], lnwar, lnx_wr[NTBB - 1]] + list(wpc_war))
                q_asm = []
                for ec in range(DC):
                    pp = ec % 2
                    woff = wpc0 + pp * 1024
                    t_wq = dma("dg",
                               ABF[:, woff:woff + 1024].rearrange("p (c x) -> p c x", c=DC),
                               bWqT.rearrange("(c p) x -> p c x", p=128)[:, :, ec * 128:(ec + 1) * 128],
                               waits=wpc_war)
                    wpc_war = []
                    bank = ec % 2
                    g = MMGroup(bank, ps[bank][:, 0:TBB], waits=[t_lnx, t_wq, t_bq, t_or])
                    for dc in range(DC):
                        g.mm(ABF[:, woff + dc * 128:woff + (dc + 1) * 128],
                             lnxbufB(dc), stop=False)
                    g.mm(ROWB[0:1, ec * 128:(ec + 1) * 128],
                         ROWB[0:1, 4096:4096 + TBB], stop=True)
                    t_mm = g.done()
                    wpc_war.append(t_mm)
                    t_raw = evac(bank, t_mm,
                                 lambda e, o=scr_raw[:, 0:TBB], i=ps[bank][:, 0:TBB]:
                                 e.activation(out=o, in_=i, func=AF.Copy), eng="scalar",
                                 waits=kv_war)
                    kv_war = []
                    dst = ABF[:, qr0 + ec * TBB:qr0 + (ec + 1) * TBB]
                    t_asm = rope_assemble(t_raw, tqb * TBB, TBB, dst, True, q_asm[-1:])
                    q_asm.append(t_asm)
                    kv_war.append(t_asm)
                    t_sq = vop(lambda e, o=scr_rope[:, 0:TBB], i=dst:
                               e.activation(out=o, in_=i, func=AF.Square),
                               waits=[t_asm], eng="scalar")
                    for hh in range(2):
                        head = 2 * ec + hh
                        gw = BK.write_waits(6) + [t_sq, t_ones]
                        gn = MMGroup(6, ps[6][0:1, 0:TBB], waits=gw)
                        gn.mm(onescol_b[64 * hh:64 * (hh + 1), 0:1],
                              scr_rope[64 * hh:64 * (hh + 1), 0:TBB], stop=True)
                        t_nm = gn.done()
                        evac(6, t_nm, lambda e, o=qmc(head), i=ps[6][0:1, 0:TBB]:
                             e.reduce_max(o, i, axis=AX.X))
                ntk = (tqb + 1) * (TBB // 128)
                t_qz0 = vop(lambda e: e.memset(ABF[:, qz0:qz0 + 2 * TBB], 0.0),
                            waits=qz_war)
                qz_war = []
                for head in range(NHEAD):
                    ec = head // 2
                    hh = head % 2
                    if hh == 0:
                        zw = [q_asm[ec], t_qz0] + qz_war
                        qz_war = []
                        za = vop(lambda e, o=ABF[0:64, qz0:qz0 + TBB],
                                 i=ABF[0:64, qr0 + ec * TBB:qr0 + (ec + 1) * TBB]:
                                 e.tensor_copy(o, i), waits=zw)
                        zb = vop(lambda e, o=ABF[64:128, qz0 + TBB:qz0 + 2 * TBB],
                                 i=ABF[64:128, qr0 + ec * TBB:qr0 + (ec + 1) * TBB]:
                                 e.tensor_copy(o, i), waits=[za])
                        t_qzr = zb
                    tb1 = vop(lambda e, o=AFP[0:1, 2088:2089], i=qmc(head),
                              s=mkc(head): e.tensor_scalar(o, i, s, None, op0=ALU.mult),
                              waits=[mks_tok[head]] + brw_war)
                    brw_war = []
                    tb2 = vop(lambda e, o=AFP[0:1, 2088:2089], i=AFP[0:1, 2088:2089]:
                              e.activation(out=o, in_=i, func=AF.Sqrt),
                              waits=[tb1], eng="scalar")
                    tb3 = vop(lambda e, o=ABF[0:1, brw0:brw0 + TBB],
                              i=ROWB[0:1, 4096:4096 + TBB], s=AFP[0:1, 2088:2089]:
                              e.activation(out=o, in_=i, func=AF.Copy, scale=s),
                              waits=[tb2, t_or], eng="scalar")
                    go = MMGroup(2, ps[2][0:64, 0:TBB], waits=[])
                    gl = MMGroup(3, ps[3][0:1, 0:TBB], waits=[t_ones])
                    for tkc in range(ntk):
                        bank = tkc % 2
                        gs = MMGroup(bank, ps[bank][:, 0:TBB],
                                     waits=[tb3, t_msk, t_idb] + ut_war[bank])
                        ut_war[bank] = []
                        gs.mm(ABF[:, ksc0 + ec * T + tkc * 128:ksc0 + ec * T + (tkc + 1) * 128],
                              ABF[:, qz0 + hh * TBB:qz0 + (hh + 1) * TBB],
                              stop=False, extra_waits=[t_qzr])
                        offs = tkc * 128 - tqb * TBB
                        if 0 <= offs < TBB:
                            gs.mm(nones_b[0:1, :], ABF[0:1, brw0:brw0 + TBB], stop=False)
                            gs.mm(ABF[:, msk0 + 896:msk0 + 1024],
                                  ABF[:, msk0 + 384 - offs:msk0 + 384 - offs + TBB],
                                  stop=True)
                        else:
                            gs.mm(nones_b[0:1, :], ABF[0:1, brw0:brw0 + TBB], stop=True)
                        t_sm = gs.done()
                        utb = ABF[:, ut0 + bank * TBB:ut0 + (bank + 1) * TBB]
                        t_ut = evac(bank, t_sm, lambda e, o=utb, i=ps[bank][:, 0:TBB]:
                                    e.activation(out=o, in_=i, func=AF.Exp), eng="scalar")
                        go.mm(ABF[:, vtok0 + tkc * D + head * HD:vtok0 + tkc * D + (head + 1) * HD],
                              utb, stop=(tkc == ntk - 1), extra_waits=[t_ut])
                        t_lread = gl.mm(onescol_b[:, 0:1], utb, stop=(tkc == ntk - 1),
                                        extra_waits=[t_ut], tag=True)
                        ut_war[bank].append(t_lread)
                    t_go = go.done()
                    t_gl = gl.done()
                    l1 = evac(3, t_gl, lambda e, o=AFP[0:1, 1024:1024 + TBB],
                              i=ps[3][0:1, 0:TBB]: e.reciprocal(o, i), waits=linv_war)
                    linv_war = []
                    gb = MMGroup(4, ps[4][0:64, 0:TBB], waits=[l1])
                    gb.mm(ones_f[0:1, 0:64], AFP[0:1, 1024:1024 + TBB], stop=True)
                    t_gb = gb.done()
                    l2 = evac(4, t_gb, lambda e, o=AFP[0:64, 1536:1536 + TBB],
                              i=ps[4][0:64, 0:TBB]: e.tensor_copy(o, i))
                    linv_war.append(l2)
                    odst = ABF[64 * hh:64 * (hh + 1), of0 + ec * TBB:of0 + (ec + 1) * TBB]
                    t_of = evac(2, t_go,
                                lambda e, o=odst, a=ps[2][0:64, 0:TBB],
                                b=AFP[0:64, 1536:1536 + TBB]: e.tensor_mul(o, a, b),
                                waits=[l2] + of_war)
                    of_war = []
                    linv_war.append(t_of)
                    brw_war.append(t_sm)
                    qz_war.append(t_sm)
                    kv_war.append(t_of)
                if DEV_ATTNDBG and tqb == 0:
                    snap1 = P.barrier()
                    dma("dg", dumps["qsc"], ABF[:, qr0:qr0 + DC * TBB], waits=snap1)
                    dma("dg", dumps["ofm"], ABF[:, of0:of0 + DC * TBB], waits=snap1)
                for e2c in range(DC):
                    pp = e2c % 2
                    woff = wpc0 + pp * 1024
                    t_wo = dma("dg",
                               ABF[:, woff:woff + 1024].rearrange("p (c x) -> p c x", c=DC),
                               bWoT.rearrange("(c p) x -> p c x", p=128)[:, :, e2c * 128:(e2c + 1) * 128],
                               waits=wpc_war)
                    wpc_war = []
                    bank = e2c % 2
                    g = MMGroup(bank, ps[bank][:, 0:TBB], waits=[t_wo, t_bo, t_or])
                    for dc in range(DC):
                        g.mm(ABF[:, woff + dc * 128:woff + (dc + 1) * 128],
                             ABF[:, of0 + dc * TBB:of0 + (dc + 1) * TBB], stop=False,
                             extra_waits=[])
                    g.mm(ROWB[0:1, 3 * D + e2c * 128:3 * D + (e2c + 1) * 128],
                         ROWB[0:1, 4096:4096 + TBB], stop=True)
                    t_mm = g.done()
                    wpc_war.append(t_mm)
                    of_war.append(t_mm)
                    rdst = R[:, e2c * T + tqb * TBB:e2c * T + (tqb + 1) * TBB]
                    evac(bank, t_mm, lambda e, o=rdst, a=ps[bank][:, 0:TBB], b=rdst:
                         e.tensor_add(o, a, b))
            return full_barrier()

        t_R_done = None
        if "attn_a" in phases:
            phase_attn_a()
        if "ffn1" in phases:
            phase_ffn()
        if "attn_b" in phases:
            phase_attn_b()
        if "ffn2" in phases:
            phase_ffn()

        def dump(name, src):
            if name in dumps:
                tok = dma("ds", dumps[name], src)
                P.emit("sync", lambda e: None, waits=[tok])

        dump("R", R[:, :])

        snap = full_barrier()
        t_st = dma("ds", out_t.rearrange("(c p) t -> p c t", p=128), Rv, waits=snap)
        P.emit("sync", lambda e: None, waits=[t_st])

        P.realize(block, sems)
    return nc


def kernel(**inputs):
    m, xts = prep_inputs(inputs)
    phases = DEV_PHASES or ("attn_a", "ffn1", "attn_b", "ffn2")
    nc = build_nc(tuple(phases))
    in_maps = [dict(m, x_t=xts[i]) for i in range(8)]
    res = run_bass_kernel_spmd(nc, in_maps, core_ids=list(range(8)))
    out = np.stack([np.ascontiguousarray(r["out_t"].T) for r in res.results], axis=0)
    return out
